# version 06
# speedup vs baseline: 1.2129x; 1.0480x over previous
import numpy as np
import ml_dtypes
import concourse.bass as bass
import concourse.mybir as mybir

F32 = mybir.dt.float32
BF16 = mybir.dt.bfloat16
AF = mybir.ActivationFunctionType
ALU = mybir.AluOpType
AX = mybir.AxisListType

D = 1024
NCC = 8
CONST_VALS = (1e-5, 1.0 / 1024, 1e-6, 8.0, 2.0, 0.5, 1.0, 0.0)


class Sem:
    def __init__(self, h, name):
        self.h = h
        self.name = name
        self.n = 0

    def tok(self):
        return (self, self.n)


class KB:
    ENGINES = ("sync", "scalar", "vector", "tensor", "gpsimd")

    def __init__(self, nc):
        self.nc = nc
        self.ops = {e: [] for e in self.ENGINES}

    def sem(self, name):
        return Sem(self.nc.alloc_semaphore(name), name)

    def emit(self, eng, fn, wait=(), inc=None, inc_n=1):
        wait = tuple((s, v) for (s, v) in wait if v > 0)
        self.ops[eng].append((wait, fn, inc, inc_n))
        if inc is not None:
            inc.n += inc_n
            return inc.tok()
        return None

    def dma(self, eng, out, in_, wait=(), inc=None, inc_n=16, **kw):
        return self.emit(eng, lambda e: e.dma_start(out=out, in_=in_, **kw),
                         wait=wait, inc=inc, inc_n=16)

    def finalize(self):
        nc = self.nc
        with nc.Block() as blk:
            for eng in self.ENGINES:
                ops = self.ops[eng]
                def make(ops):
                    def run(e):
                        for (wait, fn, inc, inc_n) in ops:
                            for (s, v) in wait:
                                e.wait_ge(s.h, v)
                            last = fn(e)
                            if inc is not None:
                                assert last is not None
                                last.then_inc(inc.h, inc_n)
                    return run
                getattr(blk, eng)(make(ops))


def build_program(S=4096, C=512, debug_out=(), phases=("p1", "p2", "a0", "m0", "a1", "m1", "p4")):
    nc = bass.Bass("TRN2", target_bir_lowering=False)
    kb = KB(nc)
    NT = C // 128
    NCH = S // C
    INV_D = 1.0 / D

    def dram_in(name, shape):
        return nc.dram_tensor(name, list(shape), F32, kind="ExternalInput").ap()

    def scratch(name, shape):
        kind = "ExternalOutput" if name in debug_out else "Internal"
        return nc.dram_tensor(name, list(shape), F32, kind=kind).ap()

    x_in = dram_in("x", (S, D))
    Wg_d = dram_in("Wg", (D, D))
    Wv_d = dram_in("Wv", (D, D))
    Wd_d = dram_in("Wd", (D, D))
    baseT_d = dram_in("baseT", (D, D))
    lorart_d = dram_in("lorart", (D, 40))
    Bp_d = dram_in("Bp", (64, D))
    bg_d = dram_in("bg_fm", (128, NCC))
    bv_d = dram_in("bv_fm", (128, NCC))
    bd_d = dram_in("bd_fm", (128, NCC))
    bb_d = dram_in("bb_fm", (128, NCC))
    rb_d = dram_in("rb_col", (8, 1))
    id_d = dram_in("ident", (128, 128))
    m1_d = dram_in("m1", (8, 64))
    m2_d = dram_in("m2", (8, 64))
    cvec_d = dram_in("cvec", (128, 8))

    WrT_d = dram_in("WrT", (D, D))
    WiT_d = dram_in("WiT", (D, D))
    WoT_d = dram_in("WoT", (D, D))
    br_d = dram_in("br_fm", (128, NCC))
    bi_d = dram_in("bi_fm", (128, NCC))
    cw_d = dram_in("convw_fm", (128, 4 * NCC))
    cb_d = dram_in("cb_fm", (128, NCC))
    la_d = dram_in("loga_fm", (128, NCC))
    gam_d = dram_in("gamma_tm", (128, D))

    wqkv_d = dram_in("wqkv", (2, D, 1536))
    wo_d = dram_in("wo", (2, 512, D))
    w1_d = dram_in("w1", (2, D, 5460))
    w2_d = dram_in("w2", (2, 2730, D))
    hsel_d = dram_in("hsel", (128, 32))
    mbc_d = dram_in("mbc", (8, 4 * 128))
    prot_d = dram_in("prot", (128, 128))
    qsd_d = dram_in("qsd", (128, 2))
    qsp_d = dram_in("qsp", (128, 2))
    ksd_d = dram_in("ksd", (128, 2))
    ksp_d = dram_in("ksp", (128, 2))
    rotA_d = dram_in("rotA", (128, S))
    rotB_d = dram_in("rotB", (128, S))
    rotC_d = dram_in("rotC", (128, S))
    rotD_d = dram_in("rotD", (128, S))
    mask_d = dram_in("mask01", (128, 1152))
    lng_d = dram_in("lng_fm", (128, 2 * NCC))
    lnb_d = dram_in("lnb_fm", (128, 2 * NCC))
    b1a_d = dram_in("b1a_fm", (128, 2 * 22))
    b1g_d = dram_in("b1g_fm", (128, 2 * 22))
    b2tm_d = dram_in("b2tm", (128, 2 * D))

    x1_d = scratch("x1", (S, D))
    xn1_d = scratch("xn1", (S, D))
    s1_d = scratch("s1", (S, D))
    ymid_d = [scratch(f"ymid{j}", (S, D)) for j in range(2)]
    yacc_d = [scratch(f"yacc{j}", (S, D)) for j in range(2)]
    hh_d = nc.dram_tensor("hh_s", [D, S], BF16, kind="Internal").ap()
    hl_d = nc.dram_tensor("hl_s", [D, S], BF16, kind="Internal").ap()
    out_d = nc.dram_tensor("out", [S, D], F32, kind="ExternalOutput").ap()

    def A(name, shape, dtype):
        return nc.alloc_sbuf_tensor(name, shape, dtype)

    NSLAB = 3
    wtmp = [A(f"wtmp{j}", [128, 1024], F32) for j in range(NSLAB)]
    wsh = [A(f"wsh{j}", [128, D], BF16) for j in range(NSLAB)]
    wsl = [A(f"wsl{j}", [128, D], BF16) for j in range(NSLAB)]
    elm = A("elm", [8, C], F32)
    elg = A("elg", [8, C], F32)
    ebh = A("ebh", [8, C], BF16)
    ebl = A("ebl", [8, C], BF16)
    gb = A("gb", [8, C], BF16)
    rm = A("rm", [64, C], F32)
    zf = A("zf", [64, C], F32)
    zh = A("zh", [64, C], BF16)
    zl = A("zl", [64, C], BF16)
    recip = A("recip", [1, C], F32)
    id_t = A("id_t", [128, 128], F32)
    m1_t = A("m1_t", [8, 64], F32)
    m2_t = A("m2_t", [8, 64], BF16)
    ones8_t = A("ones8_t", [8, 1], BF16)
    ones1x8_t = A("ones1x8_t", [1, 8], F32)
    bgf = A("bgf", [128, NCC], F32)
    bvf = A("bvf", [128, NCC], F32)
    bdf = A("bdf", [128, NCC], F32)
    bbf = A("bbf", [128, NCC], F32)
    rbc = A("rbc", [8, 1], F32)
    msum = A("msum", [128, NT], F32)
    mmean = A("mmean", [128, NT], F32)
    vsum = A("vsum", [128, NT], F32)
    vinv = A("vinv", [128, NT], F32)
    dump = A("dump", [128, D], F32)
    onecol = A("onecol", [128, 1], F32)
    cvec_t = A("cvec_t", [128, 8], F32)
    for j, v in enumerate(CONST_VALS):
        nc.const_aps.register(F32, v, cvec_t[:, j:j + 1])
    brf = A("brf", [128, NCC], F32)
    bif = A("bif", [128, NCC], F32)
    taps = A("taps", [128, 4 * NCC], F32)
    cbf = A("cbf", [128, NCC], F32)
    laf = A("laf", [128, NCC], F32)
    la8 = A("la8", [128, NCC], F32)
    zcol = A("zcol", [128, NCC], F32)

    AR_COLS = nc.sbuf_bytes_remaining // 4 - 32
    arena = A("arena", [128, AR_COLS], F32)
    ar_cur = [0]

    import collections
    AR_MAP = collections.OrderedDict()
    build_program.AR_MAP = AR_MAP

    def AR(np_, ncols, dtype=F32, _name=None):
        per = 1 if dtype == F32 else 2
        need = (ncols + per - 1) // per
        a0 = ar_cur[0]
        assert a0 + need <= AR_COLS, f"arena overflow: {a0}+{need}"
        ar_cur[0] = a0 + need + (-(a0 + need)) % 16
        v = arena[:, a0:a0 + need]
        if dtype != F32:
            v = v.bitcast(dtype)
        if np_ != 128 or v.shape[1] != ncols:
            v = v[0:np_, 0:ncols]
        import traceback
        ln = traceback.extract_stack()[-2].line or ""
        AR_MAP.setdefault((a0, need), ln[:50])
        return v

    x_tm = [AR(128, NT * D) for j in range(2)]
    xn_tm = AR(128, NT * D)
    xnh = AR(128, NCC * C, BF16)
    xnl = AR(128, NCC * C, BF16)
    SG = AR(128, NCC * C)
    TV = AR(128, NCC * C)
    S0 = AR(128, NCC * C)
    st0 = AR(128, NCC)
    lrh = AR(128, NCC * 40, BF16)
    lrl = AR(128, NCC * 40, BF16)
    bph = AR(64, D, BF16)
    bpl = AR(64, D, BF16)
    xn1f = AR(128, NCC * (C + 4))
    gma = AR(128, D)
    P12_END = ar_cur[0]
    PB = [nc.alloc_psum_tensor(f"PB{j}", [128, 512], F32) for j in range(8)]

    s_const = kb.sem("s_const")
    s_x = [kb.sem("s_x0"), kb.sem("s_x1")]
    s_w = [kb.sem(f"s_w{j}") for j in range(3)]
    s_st = [kb.sem("s_st0"), kb.sem("s_st1")]
    gpc = kb.sem("gpc")
    sc = kb.sem("sc")
    ve = kb.sem("ve")
    te = kb.sem("te")

    kb.dma("sync", id_t[:], id_d[:], inc=s_const)
    kb.dma("sync", m1_t[:], m1_d[:], inc=s_const)
    kb.dma("sync", bgf[:], bg_d[:], inc=s_const)
    kb.dma("sync", bvf[:], bv_d[:], inc=s_const)
    kb.dma("sync", bdf[:], bd_d[:], inc=s_const)
    kb.dma("sync", bbf[:], bb_d[:], inc=s_const)
    kb.dma("sync", rbc[:], rb_d[:], inc=s_const)
    kb.dma("sync", rm[0:8, 0:64], m2_d[:], inc=s_const)
    kb.dma("sync", cvec_t[:], cvec_d[:], inc=s_const)
    for kc in range(NCC):
        kb.dma("sync", wtmp[0][:, kc * 40:(kc + 1) * 40],
               lorart_d[kc * 128:(kc + 1) * 128, :], inc=s_const)
    kb.dma("sync", wtmp[1][0:64, 0:D], Bp_d[:], inc=s_const)
    consts = s_const.tok()
    kb.emit("vector", lambda e: e.memset(onecol[:], 1.0), inc=ve)
    kb.emit("scalar", lambda e: e.copy(out=ones8_t[:], in_=onecol[0:8, 0:1]),
            wait=[ve.tok()], inc=sc)
    kb.emit("scalar", lambda e: e.copy(out=m2_t[:], in_=rm[0:8, 0:64]),
            wait=[consts], inc=sc)
    kb.emit("vector", lambda e: e.memset(ones1x8_t[:], 1.0), inc=ve)
    kb.emit("scalar", lambda e: e.copy(out=lrh[:], in_=wtmp[0][:, 0:NCC * 40]),
            wait=[consts], inc=sc)
    t = sc.tok()
    kb.emit("vector", lambda e: e.tensor_tensor(out=lrl[:], in0=wtmp[0][:, 0:NCC * 40],
            in1=lrh[:], op=ALU.subtract), wait=[t], inc=ve)
    kb.emit("scalar", lambda e: e.copy(out=bph[:], in_=wtmp[1][0:64, 0:D]),
            wait=[consts], inc=sc)
    t = sc.tok()
    kb.emit("vector", lambda e: e.tensor_tensor(out=bpl[:], in0=wtmp[1][0:64, 0:D],
            in1=bph[:], op=ALU.subtract), wait=[t], inc=ve)
    ve_setup = ve.tok()

    slab_free = [ve_setup] * NSLAB
    xtm_guard = [[], []]
    prev_scan_done = None

    def ln_tm(src, dst, wait_src, eps):
        last = None
        for tt in range(NT):
            sl = slice(tt * D, (tt + 1) * D)
            t_r = kb.emit("vector", lambda e, sl=sl, tt=tt: e.reduce_sum(
                out=msum[:, tt:tt + 1], in_=src[:, sl], axis=AX.X),
                wait=wait_src if tt == 0 else (), inc=ve)
            t_q = kb.emit("scalar", lambda e, sl=sl, tt=tt: e.activation(
                out=dump[:], in_=src[:, sl], func=AF.Square,
                accum_out=vsum[:, tt:tt + 1]),
                wait=wait_src if tt == 0 else (), inc=sc)
            t_m = kb.emit("scalar", lambda e, tt=tt: e.mul(
                out=mmean[:, tt:tt + 1], in_=msum[:, tt:tt + 1], mul=INV_D),
                wait=[t_r, t_q], inc=sc)
            t_m2 = kb.emit("scalar", lambda e, tt=tt: e.activation(
                out=msum[:, tt:tt + 1], in_=mmean[:, tt:tt + 1],
                func=AF.Square), wait=[t_m], inc=sc)
            t_v = kb.emit("vector", lambda e, tt=tt: e.tensor_scalar(
                out=vsum[:, tt:tt + 1], in0=vsum[:, tt:tt + 1],
                scalar1=INV_D, scalar2=msum[:, tt:tt + 1], op0=ALU.mult,
                op1=ALU.subtract), wait=[t_m2], inc=ve)
            t_sd = kb.emit("scalar", lambda e, tt=tt, eps=eps: e.activation(
                out=vsum[:, tt:tt + 1], in_=vsum[:, tt:tt + 1], func=AF.Sqrt,
                bias=eps, scale=1.0), wait=[t_v], inc=sc)
            t_iv = kb.emit("vector", lambda e, tt=tt: e.reciprocal(
                out=vinv[:, tt:tt + 1], in_=vsum[:, tt:tt + 1]),
                wait=[t_sd], inc=ve)
            last = kb.emit("vector", lambda e, sl=sl, tt=tt: e.tensor_scalar(
                out=dst[:, sl], in0=src[:, sl], scalar1=mmean[:, tt:tt + 1],
                scalar2=vinv[:, tt:tt + 1], op0=ALU.subtract, op1=ALU.mult),
                wait=[t_iv], inc=ve)
        return last

    evac_prev = [None] * 8

    def stream_w2(wap, out_buf, act, bias_t, tail_oc=None, mm0_wait=(),
                  evac0_wait=(), rh=None, rl=None):
        nonlocal evac_prev
        rh = xnh if rh is None else rh
        rl = xnl if rl is None else rl
        for kc in range(NCC):
            sb = kc % NSLAB
            cs = slice(kc * C, (kc + 1) * C)
            kb.dma("sync", wtmp[sb][:, 0:D], wap[kc * 128:(kc + 1) * 128, :],
                   wait=[slab_free[sb]], inc=s_w[sb])
            t_l = s_w[sb].tok()
            kb.emit("gpsimd", lambda e, sb=sb: e.tensor_copy(out=wsh[sb][:],
                    in_=wtmp[sb][:, 0:D]), wait=[t_l], inc=gpc, inc_n=1)
            t_h = gpc.tok()
            kb.emit("vector", lambda e, sb=sb: e.tensor_tensor(out=wsl[sb][:],
                    in0=wtmp[sb][:, 0:D], in1=wsh[sb][:], op=ALU.subtract),
                    wait=[t_h], inc=ve)
            t_s = ve.tok()
            if kc == 0:
                for oc in range(8):
                    def mm0_fn(e, oc=oc, sb=sb, cs=cs):
                        osl = slice(oc * 128, (oc + 1) * 128)
                        e.matmul(PB[oc][:, 0:C], wsh[sb][:, osl], rh[:, cs],
                                 start=True, stop=False)
                        e.matmul(PB[oc][:, 0:C], wsh[sb][:, osl], rl[:, cs],
                                 start=False, stop=False)
                        return e.matmul(PB[oc][:, 0:C], wsl[sb][:, osl],
                                        rh[:, cs], start=False, stop=False)
                    wl = [t_s] + (list(mm0_wait) if oc == 0 else [])
                    if evac_prev[oc] is not None:
                        wl.append(evac_prev[oc])
                    kb.emit("tensor", mm0_fn, wait=wl, inc=te)
                slab_free[sb] = te.tok()
            elif kc < NCC - 1:
                def mm_fn(e, sb=sb, cs=cs):
                    last = None
                    for oc in range(8):
                        osl = slice(oc * 128, (oc + 1) * 128)
                        last = e.matmul(PB[oc][:, 0:C], wsh[sb][:, osl],
                                        rh[:, cs], start=False, stop=False)
                        last = e.matmul(PB[oc][:, 0:C], wsh[sb][:, osl],
                                        rl[:, cs], start=False, stop=False)
                        last = e.matmul(PB[oc][:, 0:C], wsl[sb][:, osl],
                                        rh[:, cs], start=False, stop=False)
                    return last
                kb.emit("tensor", mm_fn, wait=[t_s], inc=te)
                slab_free[sb] = te.tok()
            else:
                ev = None
                for oc in range(8):
                    def mmz_fn(e, oc=oc, sb=sb, cs=cs, tail_oc=tail_oc):
                        osl = slice(oc * 128, (oc + 1) * 128)
                        e.matmul(PB[oc][:, 0:C], wsh[sb][:, osl], rh[:, cs],
                                 start=False, stop=False)
                        e.matmul(PB[oc][:, 0:C], wsh[sb][:, osl], rl[:, cs],
                                 start=False, stop=False)
                        last = e.matmul(PB[oc][:, 0:C], wsl[sb][:, osl],
                                        rh[:, cs], start=False,
                                        stop=(tail_oc is None))
                        if tail_oc is not None:
                            last = tail_oc(e, oc)
                        return last
                    t_mm = kb.emit("tensor", mmz_fn,
                                   wait=[t_s] if oc == 0 else (), inc=te)
                    wle = [t_mm] + (list(evac0_wait) if oc == 0 else [])
                    ev = kb.emit("scalar", lambda e, oc=oc: e.activation(
                        out=out_buf[:, oc * C:(oc + 1) * C], in_=PB[oc][:, 0:C],
                        func=act, bias=bias_t[:, oc:oc + 1]), wait=wle, inc=sc)
                    evac_prev[oc] = ev
                slab_free[sb] = te.tok()
        return ev

    for i in (range(NCH) if "p1" in phases else ()):
        p = i % 2
        r0 = i * C

        for tt in range(NT):
            kb.dma("sync", x_tm[p][:, tt * D:(tt + 1) * D],
                   x_in[r0 + tt * 128: r0 + (tt + 1) * 128, :],
                   wait=([consts] + xtm_guard[p]) if tt == 0 else (), inc=s_x[p])
        x_here = s_x[p].tok()

        xn_done = ln_tm(x_tm[p], xn_tm, [x_here], 1e-5)

        evac_b = {0: None, 1: None}
        for c in range(NCC):
            b = c % 2
            wl = [xn_done] if c == 0 else []
            if evac_b[b] is not None:
                wl.append(evac_b[b])
            def tr_fn(e, c=c, b=b):
                last = None
                for tt in range(NT):
                    last = e.transpose(PB[b][:, tt * 128:(tt + 1) * 128],
                                       xn_tm[:, tt * D + c * 128: tt * D + (c + 1) * 128],
                                       id_t[:])
                return last
            t_tr = kb.emit("tensor", tr_fn, wait=wl, inc=te)
            csl = slice(c * C, (c + 1) * C)
            t_h = kb.emit("scalar", lambda e, csl=csl, b=b: e.copy(
                out=xnh[:, csl], in_=PB[b][:, 0:C]), wait=[t_tr], inc=sc)
            evac_b[b] = kb.emit("vector", lambda e, csl=csl, b=b: e.tensor_tensor(
                out=xnl[:, csl], in0=PB[b][:, 0:C], in1=xnh[:, csl],
                op=ALU.subtract), wait=[t_h], inc=ve)
        split_done = ve.tok()

        def lora_in_fn(e):
            last = None
            for kc in range(NCC):
                ks = slice(kc * 40, (kc + 1) * 40)
                cs = slice(kc * C, (kc + 1) * C)
                last = e.matmul(PB[0][0:40, 0:C], lrh[:, ks], xnh[:, cs],
                                start=(kc == 0), stop=False)
                last = e.matmul(PB[0][0:40, 0:C], lrh[:, ks], xnl[:, cs],
                                start=False, stop=False)
                last = e.matmul(PB[0][0:40, 0:C], lrl[:, ks], xnh[:, cs],
                                start=False, stop=(kc == NCC - 1))
            return last
        t_li = kb.emit("tensor", lora_in_fn, wait=[split_done], inc=te)
        t_mid = kb.emit("scalar", lambda e: e.copy(out=elm[:, 0:C],
                in_=PB[0][0:8, 0:C]), wait=[t_li], inc=sc)
        t_id2 = kb.emit("scalar", lambda e: e.activation(out=elg[:, 0:C],
                in_=PB[0][32:40, 0:C], func=AF.Identity, bias=rbc[:, 0:1]), inc=sc)
        t_ex = kb.emit("scalar", lambda e: e.activation(out=elg[:, 0:C],
                in_=elg[:, 0:C], func=AF.Exp), wait=[t_id2], inc=sc)
        t_eh = kb.emit("scalar", lambda e: e.copy(out=ebh[:], in_=elg[:, 0:C]),
                       wait=[t_ex], inc=sc)
        t_el = kb.emit("vector", lambda e: e.tensor_tensor(out=ebl[:],
                in0=elg[:, 0:C], in1=ebh[:], op=ALU.subtract),
                wait=[t_eh], inc=ve)
        def sum_fn(e):
            e.matmul(PB[1][0:1, 0:C], ones8_t[:], ebh[:], start=True, stop=False)
            return e.matmul(PB[1][0:1, 0:C], ones8_t[:], ebl[:], start=False,
                            stop=True)
        t_sum = kb.emit("tensor", sum_fn, wait=[t_el], inc=te)
        t_rc = kb.emit("vector", lambda e: e.reciprocal(out=recip[:],
                in_=PB[1][0:1, 0:C]), wait=[t_sum], inc=ve)
        t_bc = kb.emit("tensor", lambda e: e.matmul(PB[2][0:8, 0:C], ones1x8_t[:],
                recip[:], start=True, stop=True), wait=[t_rc], inc=te)
        t_gb = kb.emit("vector", lambda e: e.tensor_tensor(out=gb[:],
                in0=elg[:, 0:C], in1=PB[2][0:8, 0:C], op=ALU.mult),
                wait=[t_bc], inc=ve)
        t_rmm = kb.emit("tensor", lambda e: e.matmul(PB[3][0:64, 0:C], m1_t[:],
                elm[:, 0:C], start=True, stop=True), wait=[t_mid], inc=te)
        t_rme = kb.emit("scalar", lambda e: e.copy(out=rm[:, 0:C],
                in_=PB[3][0:64, 0:C]), wait=[t_rmm], inc=sc)
        t_rg = kb.emit("tensor", lambda e: e.matmul(PB[4][0:64, 0:C], m2_t[:],
                gb[:], start=True, stop=True), wait=[t_gb], inc=te)
        t_z = kb.emit("vector", lambda e: e.scalar_tensor_tensor(out=zf[:, 0:C],
                in0=rm[:, 0:C], scalar=2.0, in1=PB[4][0:64, 0:C],
                op0=ALU.mult, op1=ALU.mult), wait=[t_rg, t_rme], inc=ve)
        pb4_free = t_z
        t_zh = kb.emit("scalar", lambda e: e.copy(out=zh[:, 0:C], in_=zf[:, 0:C]),
                       wait=[t_z], inc=sc)
        z_ready = kb.emit("vector", lambda e: e.tensor_tensor(out=zl[:, 0:C],
                in0=zf[:, 0:C], in1=zh[:, 0:C], op=ALU.subtract),
                wait=[t_zh], inc=ve)

        prev_scan = [prev_scan_done] if prev_scan_done is not None else []
        all_g = stream_w2(Wg_d, SG, AF.Sigmoid, bgf,
                          mm0_wait=[pb4_free, t_rme], evac0_wait=prev_scan)
        all_v = stream_w2(Wv_d, TV, AF.Tanh, bvf, evac0_wait=prev_scan)
        xs_done = kb.emit("vector", lambda e: e.tensor_tensor(out=SG[:],
                in0=SG[:], in1=TV[:], op=ALU.mult), wait=[all_v], inc=ve)
        all_d = stream_w2(Wd_d, TV, AF.Sigmoid, bdf, evac0_wait=[xs_done])
        a_done = kb.emit("vector", lambda e: e.tensor_scalar(out=TV[:], in0=TV[:],
                scalar1=0.998, scalar2=0.001, op0=ALU.mult, op1=ALU.add),
                wait=[all_d], inc=ve)

        scan_done = None
        for c in range(NCC):
            csl = slice(c * C, (c + 1) * C)
            init = 0.0 if i == 0 else st0[:, c:c + 1]
            scan_done = kb.emit("vector", lambda e, csl=csl, init=init:
                e.tensor_tensor_scan(out=S0[:, csl], data0=TV[:, csl],
                    data1=SG[:, csl], initial=init, op0=ALU.mult, op1=ALU.add),
                wait=[a_done] if c == 0 else (), inc=ve)
        scan_done = kb.emit("vector", lambda e: e.tensor_copy(out=st0[:, 0:NCC],
                in_=S0.rearrange("q (c t) -> q c t", t=C)[:, :, C - 1:C]),
                wait=[scan_done], inc=ve)
        prev_scan_done = scan_done

        def lora_tail_oc(e, oc):
            osl = slice(oc * 128, (oc + 1) * 128)
            e.matmul(PB[oc][:, 0:C], bph[:, osl], zh[:, 0:C],
                     start=False, stop=False)
            e.matmul(PB[oc][:, 0:C], bph[:, osl], zl[:, 0:C],
                     start=False, stop=False)
            return e.matmul(PB[oc][:, 0:C], bpl[:, osl], zh[:, 0:C],
                            start=False, stop=True)
        G0 = xn_tm
        all_b = stream_w2(baseT_d, G0, AF.Silu, bbf, tail_oc=lora_tail_oc,
                          mm0_wait=[z_ready])

        prod_done = kb.emit("vector", lambda e: e.tensor_tensor(
            out=G0[:, 0:NCC * C], in0=G0[:, 0:NCC * C], in1=S0[:, 0:NCC * C],
            op=ALU.mult), wait=[scan_done, all_b], inc=ve)

        evac2 = {0: None, 1: None}
        first = True
        for tt in range(NT):
            for ch in range(2):
                b = (tt * 2 + ch) % 2
                wl = [prod_done] if first else []
                first = False
                if evac2[b] is not None:
                    wl.append(evac2[b])
                def tr2_fn(e, tt=tt, ch=ch, b=b):
                    last = None
                    for k in range(4):
                        c = ch * 4 + k
                        last = e.transpose(PB[4 + b][:, k * 128:(k + 1) * 128],
                                           G0[:, c * C + tt * 128: c * C + (tt + 1) * 128],
                                           id_t[:])
                    return last
                t_t2 = kb.emit("tensor", tr2_fn, wait=wl, inc=te)
                dsl = slice(tt * D + ch * 512, tt * D + (ch + 1) * 512)
                evac2[b] = kb.emit("vector", lambda e, dsl=dsl, b=b, p=p: e.tensor_tensor(
                    out=x_tm[p][:, dsl], in0=PB[4 + b][:, 0:512],
                    in1=x_tm[p][:, dsl], op=ALU.add), wait=[t_t2], inc=ve)
        x1_done = ve.tok()

        for tt in range(NT):
            kb.dma("gpsimd", x1_d[r0 + tt * 128: r0 + (tt + 1) * 128, :],
                   x_tm[p][:, tt * D:(tt + 1) * D],
                   wait=[x1_done] if tt == 0 else (), inc=s_st[p])
        xn1_done = ln_tm(x_tm[p], G0, [x1_done], 1e-5)
        for tt in range(NT):
            kb.dma("gpsimd", xn1_d[r0 + tt * 128: r0 + (tt + 1) * 128, :],
                   G0[:, tt * D:(tt + 1) * D],
                   wait=[xn1_done] if tt == 0 else (), inc=s_st[p])
        st_all = s_st[p].tok()
        xtm_guard[p] = [xn1_done, st_all]
        kb.emit("vector", lambda e: e.engine_nop(), wait=[st_all], inc=ve)

    kb.dma("sync", brf[:], br_d[:], wait=[s_w[0].tok(), s_w[1].tok()], inc=s_const)
    kb.dma("sync", bif[:], bi_d[:], inc=s_const)
    kb.dma("sync", taps[:], cw_d[:], inc=s_const)
    kb.dma("sync", cbf[:], cb_d[:], inc=s_const)
    kb.dma("sync", laf[:], la_d[:], inc=s_const)
    kb.dma("sync", gma[:], gam_d[:], inc=s_const)
    consts2 = s_const.tok()
    kb.emit("vector", lambda e: e.memset(zcol[:], 0.0), inc=ve)
    t = kb.emit("scalar", lambda e: e.activation(out=la8[:], in_=laf[:],
            func=AF.Sigmoid), wait=[consts2], inc=sc)
    t = kb.emit("scalar", lambda e: e.activation(out=la8[:], in_=la8[:],
            func=AF.Ln), wait=[t], inc=sc)
    la8_tok = kb.emit("scalar", lambda e: e.mul(out=la8[:], in_=la8[:], mul=8.0),
                      wait=[t], inc=sc)

    XC = SG
    RA = TV
    II = xn_tm
    HH = S0

    p1_ve_last = ve.tok()
    p1_sc_last = sc.tok()
    p1_st_final = [s_st[0].tok(), s_st[1].tok()]
    xn1tm_guard = [[], []]
    prev_convread = None
    prev_s1done = None
    prev_d1 = None
    CB = C + 4

    for i in (range(NCH) if "p2" in phases else ()):
        p = i % 2
        r0 = i * C
        for tt in range(NT):
            kb.dma("sync", x_tm[p][:, tt * D:(tt + 1) * D],
                   xn1_d[r0 + tt * 128: r0 + (tt + 1) * 128, :],
                   wait=(p1_st_final + xn1tm_guard[p]) if tt == 0 else (),
                   inc=s_x[p])
        x_here = s_x[p].tok()

        if i == 0:
            halo_tok = kb.emit("vector", lambda e: e.memset(
                xn1f.rearrange("p (c t) -> p c t", t=CB)[:, :, 0:4], 0.0),
                wait=[p1_ve_last], inc=ve)
        else:
            def halo_fn(e):
                v = xn1f.rearrange("p (c t) -> p c t", t=CB)
                return e.tensor_copy(out=v[:, :, 0:4], in_=v[:, :, CB - 4:CB])
            halo_tok = kb.emit("vector", halo_fn, wait=[prev_convread], inc=ve)

        evac_b = {0: None, 1: None}
        for c in range(NCC):
            b = c % 2
            wl = [x_here, halo_tok] if c == 0 else []
            if evac_b[b] is not None:
                wl.append(evac_b[b])
            if i == 0 and c < 2:
                wl += [p1_sc_last, p1_ve_last]
            def tr_fn(e, c=c, b=b, p=p):
                last = None
                for tt in range(NT):
                    last = e.transpose(PB[b][:, tt * 128:(tt + 1) * 128],
                                       x_tm[p][:, tt * D + c * 128: tt * D + (c + 1) * 128],
                                       id_t[:])
                return last
            t_tr = kb.emit("tensor", tr_fn, wait=wl, inc=te)
            evac_b[b] = kb.emit("vector", lambda e, c=c, b=b: e.tensor_copy(
                out=xn1f[:, c * CB + 4:(c + 1) * CB], in_=PB[b][:, 0:C]),
                wait=[t_tr], inc=ve)
        fm_done = ve.tok()

        conv_tok = prev_d1 if i > 0 else None
        for c in range(NCC):
            for k in range(4):
                tap = taps[:, c * 4 + k:c * 4 + k + 1]
                xs_ = xn1f[:, c * CB + 1 + k: c * CB + 1 + k + C]
                osl = slice(c * C, (c + 1) * C)
                wl = [fm_done] if (c == 0 and k == 0) else []
                if conv_tok is not None:
                    wl.append(conv_tok)
                if k == 0:
                    conv_tok = kb.emit("vector", lambda e, xs_=xs_, tap=tap,
                        osl=osl, c=c: e.tensor_scalar(out=XC[:, osl], in0=xs_,
                            scalar1=tap, scalar2=cbf[:, c:c + 1], op0=ALU.mult,
                            op1=ALU.add), wait=wl, inc=ve)
                else:
                    conv_tok = kb.emit("vector", lambda e, xs_=xs_, tap=tap,
                        osl=osl: e.scalar_tensor_tensor(out=XC[:, osl], in0=xs_,
                            scalar=tap, in1=XC[:, osl], op0=ALU.mult, op1=ALU.add),
                        wait=wl, inc=ve)
        prev_convread = conv_tok

        t_h = kb.emit("gpsimd", lambda e: e.tensor_copy(out=xnh[:], in_=XC[:]),
                      wait=[conv_tok], inc=gpc, inc_n=1)
        xc_split = kb.emit("vector", lambda e: e.tensor_tensor(out=xnl[:],
                in0=XC[:], in1=xnh[:], op=ALU.subtract), wait=[t_h], inc=ve)

        mm0 = [xc_split]
        if i == 0:
            mm0 += [p1_sc_last, p1_ve_last]
        ew = [prev_s1done] if prev_s1done else [p1_ve_last] + p1_st_final
        ev_r = stream_w2(WrT_d, RA, AF.Sigmoid, brf, mm0_wait=mm0, evac0_wait=ew)
        a_tok = None
        for c in range(NCC):
            osl = slice(c * C, (c + 1) * C)
            t = kb.emit("vector", lambda e, osl=osl, c=c: e.tensor_scalar(
                out=RA[:, osl], in0=RA[:, osl], scalar1=la8[:, c:c + 1],
                scalar2=None, op0=ALU.mult),
                wait=[ev_r, la8_tok] if c == 0 else (), inc=ve)
            a_tok = kb.emit("scalar", lambda e, osl=osl: e.activation(
                out=RA[:, osl], in_=RA[:, osl], func=AF.Exp), wait=[t], inc=sc)

        ev_i = stream_w2(WiT_d, II, AF.Sigmoid, bif, evac0_wait=ew)
        inp_tok = kb.emit("vector", lambda e: e.tensor_tensor(out=II[:],
                in0=II[:], in1=XC[:], op=ALU.mult), wait=[ev_i], inc=ve)

        t = kb.emit("scalar", lambda e: e.activation(out=XC[:], in_=RA[:],
                func=AF.Square), wait=[a_tok, inp_tok], inc=sc)
        t = kb.emit("vector", lambda e: e.tensor_scalar(out=XC[:], in0=XC[:],
                scalar1=-1.0, scalar2=1.0, op0=ALU.mult, op1=ALU.add),
                wait=[t], inc=ve)
        t = kb.emit("vector", lambda e: e.tensor_scalar_max(out=XC[:], in0=XC[:],
                scalar1=0.0), wait=[t], inc=ve)
        t = kb.emit("scalar", lambda e: e.activation(out=XC[:], in_=XC[:],
                func=AF.Sqrt), wait=[t], inc=sc)
        d1_tok = kb.emit("vector", lambda e: e.tensor_tensor(out=II[:],
                in0=II[:], in1=XC[:], op=ALU.mult), wait=[t], inc=ve)
        prev_d1 = d1_tok

        scan2 = None
        for c in range(NCC):
            csl = slice(c * C, (c + 1) * C)
            init = 0.0 if i == 0 else st0[:, c:c + 1]
            scan2 = kb.emit("vector", lambda e, csl=csl, init=init:
                e.tensor_tensor_scan(out=HH[:, csl], data0=RA[:, csl],
                    data1=II[:, csl], initial=init, op0=ALU.mult, op1=ALU.add),
                wait=[d1_tok] if c == 0 else (), inc=ve)
        scan2 = kb.emit("vector", lambda e: e.tensor_copy(out=st0[:, 0:NCC],
                in_=HH.rearrange("q (c t) -> q c t", t=C)[:, :, C - 1:C]),
                wait=[scan2], inc=ve)

        t_h = kb.emit("gpsimd", lambda e: e.tensor_copy(out=xnh[:],
                in_=HH[:]), wait=[scan2], inc=gpc, inc_n=1)
        h_split = kb.emit("vector", lambda e: e.tensor_tensor(out=xnl[:],
                in0=HH[:], in1=xnh[:], op=ALU.subtract), wait=[t_h], inc=ve)
        ev_o = stream_w2(WoT_d, II, AF.Identity, zcol, mm0_wait=[h_split],
                         evac0_wait=[scan2])

        evac2 = {0: None, 1: None}
        first = True
        for tt in range(NT):
            for ch in range(2):
                b = (tt * 2 + ch) % 2
                wl = [ev_o] if first else []
                first = False
                if evac2[b] is not None:
                    wl.append(evac2[b])
                def tr2_fn(e, tt=tt, ch=ch, b=b):
                    last = None
                    for k in range(4):
                        c = ch * 4 + k
                        last = e.transpose(PB[4 + b][:, k * 128:(k + 1) * 128],
                                           II[:, c * C + tt * 128: c * C + (tt + 1) * 128],
                                           id_t[:])
                    return last
                t_t2 = kb.emit("tensor", tr2_fn, wait=wl, inc=te)
                dsl = slice(tt * D + ch * 512, tt * D + (ch + 1) * 512)
                evac2[b] = kb.emit("vector", lambda e, dsl=dsl, b=b: e.tensor_copy(
                    out=RA[:, dsl], in_=PB[4 + b][:, 0:512]), wait=[t_t2], inc=ve)
        o_tm = ve.tok()
        for tt in range(NT):
            sl = slice(tt * D, (tt + 1) * D)
            t = kb.emit("scalar", lambda e, sl=sl, tt=tt: e.activation(
                out=dump[:], in_=RA[:, sl], func=AF.Square,
                accum_out=vsum[:, tt:tt + 1]),
                wait=[o_tm] if tt == 0 else (), inc=sc)
            t = kb.emit("scalar", lambda e, tt=tt: e.activation(
                out=mmean[:, tt:tt + 1], in_=vsum[:, tt:tt + 1], func=AF.Sqrt,
                bias=1e-6, scale=INV_D), wait=[t], inc=sc)
            t = kb.emit("vector", lambda e, tt=tt: e.reciprocal(
                out=vinv[:, tt:tt + 1], in_=mmean[:, tt:tt + 1]), wait=[t], inc=ve)
            t = kb.emit("vector", lambda e, sl=sl, tt=tt, p=p:
                e.scalar_tensor_tensor(out=x_tm[p][:, sl], in0=RA[:, sl],
                    scalar=vinv[:, tt:tt + 1], in1=gma[:], op0=ALU.mult,
                    op1=ALU.mult), wait=[t], inc=ve)
        s1_done = ve.tok()
        prev_s1done = s1_done
        for tt in range(NT):
            kb.dma("gpsimd", s1_d[r0 + tt * 128: r0 + (tt + 1) * 128, :],
                   x_tm[p][:, tt * D:(tt + 1) * D],
                   wait=[s1_done] if tt == 0 else (), inc=s_st[p])
        xn1tm_guard[p] = [s1_done, s_st[p].tok()]


    CA = 256
    NJ = S // CA
    ar_cur[0] = 0
    y_tm = [AR(128, 2 * D) for _ in range(2)]
    y_fm = AR(128, NCC * CA)
    y2_tm = y_fm
    SHh = AR(128, NCC * CA, BF16)
    SHl = AR(128, NCC * CA, BF16)
    Qf = AR(128, 4 * CA)
    Kf = AR(128, 4 * CA)
    Qh = AR(128, 4 * CA, BF16)
    Ql = AR(128, 4 * CA, BF16)
    Krh = [AR(128, 4 * CA, BF16) for _ in range(3)]
    Krl = [AR(128, 4 * CA, BF16) for _ in range(3)]
    Vrh = [AR(128, 2 * 520, BF16) for _ in range(3)]
    Vrl = [AR(128, 2 * 520, BF16) for _ in range(3)]
    rott = [AR(128, CA) for _ in range(6)]
    Et_t = [AR(128, CA) for _ in range(3)]
    Emt_t = [AR(128, CA)]
    Sh_t = [AR(128, CA, BF16) for _ in range(12)]
    Sl_t = [AR(128, CA, BF16) for _ in range(12)]
    Obuf = Qf
    _kfb = Kf.bitcast(BF16)
    Oh = _kfb[:, 0:4 * CA]
    Ol = _kfb[:, 4 * CA:8 * CA]
    wqh = AR(128, NCC * 1536, BF16)
    wql = AR(128, NCC * 1536, BF16)
    woh = AR(128, 4 * D, BF16)
    wol = AR(128, 4 * D, BF16)
    maskb = AR(128, 1152, BF16)
    P3A_END = ar_cur[0]
    hsel_t = AR(128, 32)
    mbc_t = AR(8, 4 * 128)
    prot0 = AR(128, 128)
    protq = AR(128, 128)
    protk = AR(128, 128)
    qsd_t = AR(128, 2)
    ksd_t = AR(128, 2)
    den_all = y_fm[0:1, 0:8 * CA]
    ssq = Et_t[0][0:8, 0:CA]
    qin = Et_t[1][0:8, 0:CA]
    kin = Et_t[2][0:8, 0:CA]
    hcar = AR(128, 4)
    onesr = AR(1, 128)

    extra_barrier_sems = []

    def phase_barrier():
        toks = [sc.tok(), ve.tok(), te.tok(), gpc.tok(),
                s_st[0].tok(), s_st[1].tok(), s_x[0].tok(), s_x[1].tok(),
                s_w[0].tok(), s_w[1].tok(), s_const.tok()]
        toks += [s.tok() for s in extra_barrier_sems]
        for eng, dn in (("scalar", sc), ("vector", ve), ("gpsimd", gpc)):
            kb.emit(eng, (lambda e: e.nop()) if eng == "scalar" else
                    (lambda e: e.engine_nop()), wait=toks, inc=dn)
        t2 = [sc.tok(), ve.tok(), gpc.tok()]
        for eng in ("scalar", "vector", "gpsimd", "tensor", "sync"):
            dn = {"scalar": sc, "vector": ve, "gpsimd": gpc, "tensor": te,
                  "sync": None}[eng]
            kb.emit(eng, (lambda e: e.engine_nop()) if eng in ("vector", "gpsimd")
                    else (lambda e: e.nop()), wait=t2, inc=dn)
        return t2

    def load_split(dst_h, dst_l, dram_ap, rows, cols, dcol0=0, wait_extra=(),
                   bstride=None):
        toks = []
        bstride = cols if bstride is None else bstride
        nkc = rows // 128
        for kc in range(nkc):
            sb = kc % NSLAB
            wl = [slab_free[sb]] + (list(wait_extra) if kc < 2 else [])
            kb.dma("sync", wtmp[sb][:, 0:cols],
                   dram_ap[kc * 128:(kc + 1) * 128, :], wait=wl, inc=s_w[sb])
            t_l = s_w[sb].tok()
            csl = slice(dcol0 + kc * bstride, dcol0 + kc * bstride + cols)
            kb.emit("gpsimd", lambda e, sb=sb, csl=csl, cols=cols: e.tensor_copy(
                out=dst_h[:, csl], in_=wtmp[sb][:, 0:cols]), wait=[t_l],
                inc=gpc, inc_n=1)
            t_h = gpc.tok()
            t_s = kb.emit("vector", lambda e, sb=sb, csl=csl, cols=cols:
                e.tensor_tensor(out=dst_l[:, csl], in0=wtmp[sb][:, 0:cols],
                    in1=dst_h[:, csl], op=ALU.subtract), wait=[t_h], inc=ve)
            slab_free[sb] = t_s
            toks.append(t_s)
        return toks[-1]

    def emit_attn(l, src_kind):
        bar = phase_barrier()
        kb.dma("sync", hsel_t[:], hsel_d[:], wait=bar, inc=s_const)
        kb.dma("sync", mbc_t[:], mbc_d[:], inc=s_const)
        kb.dma("sync", prot0[:], prot_d[:], inc=s_const)
        kb.dma("sync", qsd_t[:], qsd_d[:], inc=s_const)
        kb.dma("sync", ksd_t[:], ksd_d[:], inc=s_const)
        kb.dma("sync", wtmp[0][:, 0:576], mask_d[:, 0:576],
               wait=[slab_free[0]], inc=s_const)
        kb.dma("sync", wtmp[1][:, 0:576], mask_d[:, 576:1152],
               wait=[slab_free[1]], inc=s_const)
        c2 = s_const.tok()
        kb.emit("scalar", lambda e: e.copy(out=maskb[:, 0:576],
                in_=wtmp[0][:, 0:576]), wait=[c2], inc=sc)
        t_m = kb.emit("scalar", lambda e: e.copy(out=maskb[:, 576:1152],
                in_=wtmp[1][:, 0:576]), inc=sc)
        kb.emit("vector", lambda e: e.memset(onesr[:], 1.0), wait=bar, inc=ve)
        kb.emit("vector", lambda e: e.memset(hcar[:], 0.0), inc=ve)
        t_pq = kb.emit("vector", lambda e, l=l: e.tensor_scalar(out=protq[:],
                in0=prot0[:], scalar1=qsd_t[:, l:l + 1], scalar2=None,
                op0=ALU.mult), wait=[c2], inc=ve)
        t_pk = kb.emit("vector", lambda e, l=l: e.tensor_scalar(out=protk[:],
                in0=prot0[:], scalar1=ksd_t[:, l:l + 1], scalar2=None,
                op0=ALU.mult), inc=ve)
        for sl_ in range(3):
            kb.emit("vector", lambda e, sl_=sl_: e.memset(
                Vrh[sl_].rearrange("p (t h c) -> p t h c", t=2, h=8)[:, :, :, 64:65], 1.0), inc=ve)
            kb.emit("vector", lambda e, sl_=sl_: e.memset(
                Vrl[sl_].rearrange("p (t h c) -> p t h c", t=2, h=8)[:, :, :, 64:65], 0.0), inc=ve)
        setup_ve = ve.tok()
        load_split(wqh, wql, wqkv_d[l][:, 0:768], D, 768,
                   wait_extra=bar + [t_m], bstride=1536)
        t_wq = load_split(wqh, wql, wqkv_d[l][:, 768:1536], D, 768, dcol0=768,
                          bstride=1536)
        t_wo = load_split(woh, wol, wo_d[l], 512, D)
        wq_gp = gpc.tok()

        prev_attn_te = None
        prev_sh_gp = None
        prev_split3 = None
        prev_o_ve = None
        prev_o_gp = None
        prev_exp_last = None
        prev_qk_rot = None
        prev_vevac = []
        prev_odone = None
        prev_car = None
        et_last = {}
        emt_last = {}
        yguard = [[], []]
        for j in range(NJ):
            p = j % 2
            q0 = j * CA
            slot = j % 3
            wl0 = yguard[p] + ([setup_ve] if j == 0 else [])
            if src_kind == "xn1":
                for tt in range(2):
                    kb.dma("sync", y_tm[p][:, tt * D:(tt + 1) * D],
                           xn1_d[q0 + tt * 128: q0 + (tt + 1) * 128, :],
                           wait=wl0 if tt == 0 else (), inc=s_x[p])
            else:
                for tt in range(2):
                    kb.dma("sync", y_tm[p][:, tt * D:(tt + 1) * D],
                           ymid_d[0][q0 + tt * 128: q0 + (tt + 1) * 128, :],
                           wait=wl0 if tt == 0 else (), inc=s_x[p])
                for tt in range(2):
                    kb.dma("sync", y2_tm[:, tt * D:(tt + 1) * D],
                           yacc_d[0][q0 + tt * 128: q0 + (tt + 1) * 128, :],
                           wait=([prev_attn_te, prev_o_ve] if (tt == 0 and
                           prev_o_ve is not None) else ()), inc=s_x[p])
            kb.dma("sync", rott[p][:, 0:CA], rotA_d[:, q0:q0 + CA], inc=s_x[p])
            kb.dma("sync", rott[2 + p][:, 0:CA], rotC_d[:, q0:q0 + CA], inc=s_x[p])
            wbd = [prev_qk_rot] if prev_qk_rot is not None else []
            kb.dma("sync", rott[4][:, 0:CA], rotB_d[:, q0:q0 + CA],
                   wait=wbd, inc=s_x[p])
            kb.dma("sync", rott[5][:, 0:CA], rotD_d[:, q0:q0 + CA], inc=s_x[p])
            ld = s_x[p].tok()
            if src_kind == "prev":
                yready = kb.emit("vector", lambda e, p=p: e.tensor_tensor(
                    out=y_tm[p][:], in0=y_tm[p][:], in1=y2_tm[:], op=ALU.add),
                    wait=[ld], inc=ve)
            else:
                yready = ld
            t = kb.emit("vector", lambda e, l=l, p=p: e.tensor_scalar(
                out=rott[p][:], in0=rott[p][:], scalar1=qsd_t[:, l:l + 1],
                scalar2=None, op0=ALU.mult), wait=[ld, t_pq, t_pk], inc=ve)
            rotf = kb.emit("vector", lambda e, l=l, p=p: e.tensor_scalar(
                out=rott[2 + p][:], in0=rott[2 + p][:],
                scalar1=ksd_t[:, l:l + 1], scalar2=None, op0=ALU.mult),
                wait=[t], inc=ve)

            evb = {0: None, 1: None}
            for c in range(NCC):
                b = c % 2
                wlt = []
                if c == 0:
                    wlt += [yready if isinstance(yready, tuple) else ld]
                    if prev_attn_te is not None:
                        wlt.append(prev_attn_te)
                    wlt += prev_vevac
                if evb[b] is not None:
                    wlt.append(evb[b])
                def trf(e, c=c, b=b, p=p):
                    last = None
                    for tt in range(2):
                        last = e.transpose(PB[b][:, tt * 128:(tt + 1) * 128],
                                           y_tm[p][:, tt * D + c * 128: tt * D + (c + 1) * 128],
                                           id_t[:])
                    return last
                t_tr = kb.emit("tensor", trf, wait=wlt, inc=te)
                wle = [t_tr]
                if c == 0 and prev_sh_gp is not None:
                    wle.append(prev_sh_gp)
                evb[b] = kb.emit("vector", lambda e, c=c, b=b: e.tensor_copy(
                    out=y_fm[:, c * CA:(c + 1) * CA], in_=PB[b][:, 0:CA]),
                    wait=wle, inc=ve)
            yfm_done = ve.tok()

            wsh0 = [yfm_done, wq_gp]
            if prev_attn_te is not None:
                wsh0.append(prev_attn_te)
            yv3 = y_fm.rearrange("q (c t) -> q c t", t=CA)
            hv3h = SHh.rearrange("q (c t) -> q c t", t=CA)
            hv3l = SHl.rearrange("q (c t) -> q c t", t=CA)
            t_h = kb.emit("gpsimd", lambda e: e.tensor_copy(
                out=SHh[:, 0:4 * CA], in_=y_fm[:, 0:4 * CA]),
                wait=wsh0, inc=gpc, inc_n=1)
            kb.emit("gpsimd", lambda e, yv3=yv3, hv3h=hv3h: e.tensor_copy(
                out=hv3h[:, 4:8, 1:CA], in_=yv3[:, 4:8, 0:CA - 1]),
                inc=gpc, inc_n=1)
            kb.emit("gpsimd", lambda e, hv3h=hv3h: e.tensor_copy(
                out=hv3h[:, 4:8, 0:1],
                in_=hcar.rearrange("q (c o) -> q c o", o=1)[:, 0:4, :]),
                inc=gpc, inc_n=1)
            t_hh = gpc.tok()
            kb.emit("vector", lambda e: e.tensor_tensor(
                out=SHl[:, 0:4 * CA], in0=y_fm[:, 0:4 * CA],
                in1=SHh[:, 0:4 * CA], op=ALU.subtract), wait=[t_hh], inc=ve)
            kb.emit("vector", lambda e, yv3=yv3, hv3l=hv3l, hv3h=hv3h:
                e.tensor_tensor(out=hv3l[:, 4:8, 1:CA], in0=yv3[:, 4:8, 0:CA - 1],
                    in1=hv3h[:, 4:8, 1:CA], op=ALU.subtract), inc=ve)
            kb.emit("vector", lambda e, hv3l=hv3l, hv3h=hv3h: e.tensor_tensor(
                out=hv3l[:, 4:8, 0:1],
                in0=hcar.rearrange("q (c o) -> q c o", o=1)[:, 0:4, :],
                in1=hv3h[:, 4:8, 0:1], op=ALU.subtract), inc=ve)
            sh_done = ve.tok()
            sh_gp = gpc.tok()
            prev_sh_gp = sh_gp
            car_tok = kb.emit("vector", lambda e: e.tensor_copy(
                out=hcar[:, 0:4],
                in_=y_fm.rearrange("p (c t) -> p c t", t=CA)[:, 4:8, CA - 1:CA]),
                wait=[sh_done, sh_gp], inc=ve)
            prev_car = car_tok

            ev_qk = {}
            for oc in range(8):
                col0 = 0 if oc < 4 else 512
                ow = oc if oc < 4 else oc - 4
                def qk_fn(e, oc=oc, col0=col0, ow=ow):
                    last = None
                    for kc in range(NCC):
                        wsl_ = slice(kc * 1536 + col0 + ow * 128,
                                     kc * 1536 + col0 + (ow + 1) * 128)
                        rs = slice(kc * CA, (kc + 1) * CA)
                        last = e.matmul(PB[oc][:, 0:CA], wqh[:, wsl_], SHh[:, rs],
                                        start=(kc == 0), stop=False)
                        last = e.matmul(PB[oc][:, 0:CA], wqh[:, wsl_], SHl[:, rs],
                                        start=False, stop=False)
                        last = e.matmul(PB[oc][:, 0:CA], wql[:, wsl_], SHh[:, rs],
                                        start=False, stop=(kc == NCC - 1))
                    return last
                wlq = [sh_done, t_wq] if oc == 0 else []
                t_mm = kb.emit("tensor", qk_fn, wait=wlq, inc=te)
                dst = Qf if oc < 4 else Kf
                wlev = [t_mm]
                if oc == 0 and prev_split3 is not None:
                    wlev.append(prev_split3)
                ev_qk[oc] = kb.emit("scalar", lambda e, oc=oc, ow=ow, dst=dst:
                    e.copy(out=dst[:, ow * CA:(ow + 1) * CA], in_=PB[oc][:, 0:CA]),
                    wait=wlev, inc=sc)
            qk_done = sc.tok()

            for tt in range(2):
                def v_fn(e, tt=tt):
                    last = None
                    for kc in range(NCC):
                        ls = slice(kc * CA + tt * 128, kc * CA + tt * 128 + 128)
                        ws = slice(kc * 1536 + 1024, kc * 1536 + 1536)
                        last = e.matmul(PB[tt][:, 0:512], SHh[:, ls], wqh[:, ws],
                                        start=(kc == 0), stop=False)
                        last = e.matmul(PB[tt][:, 0:512], SHl[:, ls], wqh[:, ws],
                                        start=False, stop=False)
                        last = e.matmul(PB[tt][:, 0:512], SHh[:, ls], wql[:, ws],
                                        start=False, stop=(kc == NCC - 1))
                    return last
                wlv = [ev_qk[0], ev_qk[1]] if tt == 0 else []
                t_mm = kb.emit("tensor", v_fn, wait=wlv, inc=te)
                vdst_h = Vrh[slot].rearrange("p (t h c) -> p t h c", t=2, h=8)[:, tt, :, 0:64]
                vdst_l = Vrl[slot].rearrange("p (t h c) -> p t h c", t=2, h=8)[:, tt, :, 0:64]
                vsrc = PB[tt].rearrange("p (h c) -> p h c", h=8)
                wv = [t_mm]
                if prev_attn_te is not None:
                    wv.append(prev_attn_te)
                t_vh = kb.emit("scalar", lambda e, vdst_h=vdst_h, vsrc=vsrc:
                    e.copy(out=vdst_h, in_=vsrc[:, :, 0:64]), wait=wv, inc=sc)
                kb.emit("vector", lambda e, vdst_l=vdst_l, vdst_h=vdst_h,
                    vsrc=vsrc: e.tensor_tensor(out=vdst_l, in0=vsrc[:, :, 0:64],
                        in1=vdst_h, op=ALU.subtract), wait=[t_vh], inc=ve)
            v_done = ve.tok()
            prev_vevac = [v_done, gpc.tok()]

            def ss_fn(e, bank):
                last = None
                for g in range(4):
                    last = e.matmul(PB[bank][0:8, 0:CA], hsel_t[:, g * 8:(g + 1) * 8],
                                    dump[:, g * CA:(g + 1) * CA],
                                    start=(g == 0), stop=(g == 3))
                return last
            sqq = kb.emit("scalar", lambda e: e.activation(out=dump[:, 0:4 * CA],
                    in_=Qf[:], func=AF.Square), wait=[qk_done], inc=sc)
            t_q2 = kb.emit("tensor", lambda e: ss_fn(e, 2),
                           wait=[sqq, v_done], inc=te)
            sqk = kb.emit("scalar", lambda e: e.activation(out=dump[:, 0:4 * CA],
                    in_=Kf[:], func=AF.Square), wait=[t_q2], inc=sc)
            t_k2 = kb.emit("tensor", lambda e: ss_fn(e, 3), wait=[sqk], inc=te)
            t = kb.emit("scalar", lambda e: e.activation(out=ssq[:], in_=PB[2][0:8, 0:CA],
                    func=AF.Sqrt), wait=[t_q2], inc=sc)
            t = kb.emit("vector", lambda e: e.tensor_scalar_max(out=qin[:],
                    in0=ssq[:], scalar1=1e-12), wait=[t], inc=ve)
            t_qi = kb.emit("vector", lambda e: e.reciprocal(out=qin[:], in_=qin[:]),
                    wait=[t], inc=ve)
            t = kb.emit("scalar", lambda e: e.activation(out=ssq[:], in_=PB[3][0:8, 0:CA],
                    func=AF.Sqrt), wait=[t_k2, t_qi], inc=sc)
            t = kb.emit("vector", lambda e: e.tensor_scalar_max(out=kin[:],
                    in0=ssq[:], scalar1=1e-12), wait=[t], inc=ve)
            t_ki = kb.emit("vector", lambda e: e.reciprocal(out=kin[:], in_=kin[:]),
                    wait=[t], inc=ve)

            rot_done = []
            first_rot = True
            for (src_f, inv_t, prot_t, ra, rb, t_inv) in (
                    (Qf, qin, protq, rott[p], rott[4], t_qi),
                    (Kf, kin, protk, rott[2 + p], rott[5], t_ki)):
                for g in range(4):
                    gs = slice(g * CA, (g + 1) * CA)
                    t_bc = kb.emit("tensor", lambda e, g=g, inv_t=inv_t: e.matmul(
                        PB[4][:, 0:CA], mbc_t[:, g * 128:(g + 1) * 128], inv_t[:],
                        start=True, stop=True), wait=[t_inv], inc=te)
                    t_n = kb.emit("vector", lambda e, gs=gs, src_f=src_f: e.tensor_tensor(
                        out=src_f[:, gs], in0=src_f[:, gs], in1=PB[4][:, 0:CA],
                        op=ALU.mult), wait=[t_bc], inc=ve)
                    t_rm = kb.emit("tensor", lambda e, gs=gs, src_f=src_f,
                        prot_t=prot_t: e.matmul(PB[5][:, 0:CA], prot_t[:],
                            src_f[:, gs], start=True, stop=True), wait=[t_n], inc=te)
                    wl1 = [t_rm, rotf]
                    if first_rot and prev_exp_last is not None:
                        wl1.append(prev_exp_last)
                    first_rot = False
                    t_1 = kb.emit("vector", lambda e, gs=gs, src_f=src_f, ra=ra:
                        e.tensor_tensor(out=Et_t[0][:], in0=src_f[:, gs], in1=ra[:],
                            op=ALU.mult), wait=wl1, inc=ve)
                    t_2 = kb.emit("vector", lambda e, rb=rb: e.tensor_tensor(
                        out=Emt_t[0][:], in0=PB[5][:, 0:CA], in1=rb[:],
                        op=ALU.mult), wait=[t_1], inc=ve)
                    rot_done.append(kb.emit("vector", lambda e, gs=gs, src_f=src_f:
                        e.tensor_tensor(out=src_f[:, gs], in0=Et_t[0][:],
                            in1=Emt_t[0][:], op=ALU.add), wait=[t_2], inc=ve))
            qk_rot = ve.tok()
            prev_qk_rot = qk_rot
            et_last = {0: qk_rot, 1: qk_rot, 2: qk_rot}

            t_h = kb.emit("gpsimd", lambda e: e.tensor_copy(out=Qh[:], in_=Qf[:]),
                          wait=[qk_rot], inc=gpc, inc_n=1)
            kb.emit("vector", lambda e: e.tensor_tensor(out=Ql[:], in0=Qf[:],
                    in1=Qh[:], op=ALU.subtract), wait=[gpc.tok()], inc=ve)
            wk = [qk_rot]
            if prev_attn_te is not None:
                wk.append(prev_attn_te)
            t_h = kb.emit("gpsimd", lambda e, slot=slot: e.tensor_copy(
                out=Krh[slot][:], in_=Kf[:]), wait=wk, inc=gpc, inc_n=1)
            kb.emit("vector", lambda e, slot=slot: e.tensor_tensor(
                out=Krl[slot][:], in0=Kf[:], in1=Krh[slot][:], op=ALU.subtract),
                wait=[gpc.tok()], inc=ve)
            split_done3 = ve.tok()
            prev_split3 = split_done3

            kidx_list = []
            for kx in range(6):
                kbase = q0 - 512 + kx * 128
                if kbase < 0:
                    continue
                kslot = (kbase // CA) % 3
                ktt = (kbase % CA) // 128
                moff = 384 + (q0 - kbase)
                kidx_list.append((kbase, kslot, ktt, moff))
            NKI = len(kidx_list)
            exp_tok = {}
            sp_tok = {}
            av_tok = {}
            ev_o_tok = {}

            def s_mms(h):
                bs = 0 if h % 2 == 0 else 3
                ts0 = (h % 2) * 6
                hr0 = (h % 2) * 64
                g = h // 2
                for ki, (kbase, kslot, ktt, moff) in enumerate(kidx_list):
                    bank = bs + ki % 3
                    tsl = ts0 + ki
                    kc_sl = slice(g * CA + ktt * 128, g * CA + ktt * 128 + 128)
                    def sfn(e, h=h, bank=bank, kslot=kslot, kc_sl=kc_sl, hr0=hr0, g=g):
                        e.matmul(PB[bank][:, 0:CA],
                                 Krh[kslot][hr0:hr0 + 64, kc_sl],
                                 Qh[hr0:hr0 + 64, g * CA:(g + 1) * CA],
                                 start=True, stop=False)
                        e.matmul(PB[bank][:, 0:CA],
                                 Krh[kslot][hr0:hr0 + 64, kc_sl],
                                 Ql[hr0:hr0 + 64, g * CA:(g + 1) * CA],
                                 start=False, stop=False)
                        return e.matmul(PB[bank][:, 0:CA],
                                 Krl[kslot][hr0:hr0 + 64, kc_sl],
                                 Qh[hr0:hr0 + 64, g * CA:(g + 1) * CA],
                                 start=False, stop=True)
                    wls = []
                    if h <= 1 and ki == 0:
                        wls = [split_done3]
                    if (h - 2, ki) in exp_tok:
                        wls.append(exp_tok[(h - 2, ki)])
                    if (h, ki - 3) in exp_tok:
                        wls.append(exp_tok[(h, ki - 3)])
                    t_s = kb.emit("tensor", sfn, wait=wls, inc=te)
                    eb = ki % 3
                    we = [t_s]
                    if eb in et_last:
                        we.append(et_last[eb])
                    exp_tok[(h, ki)] = kb.emit("scalar", lambda e, bank=bank, eb=eb:
                        e.activation(out=Et_t[eb][:], in_=PB[bank][:, 0:CA],
                            func=AF.Exp, scale=8.0), wait=we, inc=sc)
                    wh_ = [exp_tok[(h, ki)]]
                    if h >= 2 and (h - 2) in av_tok:
                        wh_.append(av_tok[h - 2])
                    t_hi = kb.emit("scalar", lambda e, eb=eb, tsl=tsl: e.copy(
                        out=Sh_t[tsl][:], in_=Et_t[eb][:]), wait=wh_, inc=sc)
                    t_lo = kb.emit("vector", lambda e, eb=eb, tsl=tsl:
                        e.tensor_tensor(out=Sl_t[tsl][:], in0=Et_t[eb][:],
                            in1=Sh_t[tsl][:], op=ALU.subtract),
                        wait=[t_hi, t_m], inc=ve)
                    et_last[eb] = t_lo
                    dmin = (q0 - kbase) - 127
                    dmax = (q0 - kbase) + CA - 1
                    if dmin >= 0 and dmax <= 512:
                        sp_tok[(h, ki)] = t_lo
                    else:
                        t_mh = kb.emit("gpsimd", lambda e, tsl=tsl, moff=moff:
                            e.tensor_tensor(out=Sh_t[tsl][:], in0=Sh_t[tsl][:],
                                in1=maskb[:, moff:moff + CA], op=ALU.mult),
                            wait=[t_lo], inc=gpc, inc_n=1)
                        sp_tok[(h, ki)] = kb.emit("gpsimd", lambda e, tsl=tsl,
                            moff=moff: e.tensor_tensor(out=Sl_t[tsl][:],
                                in0=Sl_t[tsl][:], in1=maskb[:, moff:moff + CA],
                                op=ALU.mult), wait=[t_mh], inc=gpc, inc_n=1)

            def av_mms(h):
                bank = 6 + h % 2
                ts0 = (h % 2) * 6
                t_av = None
                for ki, (kbase, kslot, ktt, moff) in enumerate(kidx_list):
                    def afn(e, h=h, bank=bank, ts0=ts0, ki=ki, kslot=kslot,
                            ktt=ktt, NKI=NKI):
                        vh = Vrh[kslot][:, ktt * 520 + h * 65: ktt * 520 + (h + 1) * 65]
                        vl = Vrl[kslot][:, ktt * 520 + h * 65: ktt * 520 + (h + 1) * 65]
                        e.matmul(PB[bank][0:65, 0:CA], vh, Sh_t[ts0 + ki][:],
                                 start=(ki == 0), stop=False)
                        e.matmul(PB[bank][0:65, 0:CA], vh, Sl_t[ts0 + ki][:],
                                 start=False, stop=False)
                        return e.matmul(PB[bank][0:65, 0:CA], vl, Sh_t[ts0 + ki][:],
                                        start=False, stop=(ki == NKI - 1))
                    wla = [sp_tok[(h, ki)]]
                    if ki == 0:
                        if (h - 2) in av_tok:
                            wla.append(ev_o_tok[h - 2])
                        if h <= 1 and prev_odone is not None:
                            wla.append(prev_odone)
                    t_av = kb.emit("tensor", afn, wait=wla, inc=te)
                av_tok[h] = t_av
                g = h // 2
                hr0 = (h % 2) * 64
                wle1 = [t_av]
                if h <= 1 and prev_o_ve is not None:
                    wle1 += [prev_o_ve, prev_o_gp, prev_attn_te]
                t_e1 = kb.emit("scalar", lambda e, bank=bank, g=g, hr0=hr0, h=h:
                    e.copy(out=Obuf[hr0:hr0 + 64, g * CA:(g + 1) * CA],
                           in_=PB[bank][0:64, 0:CA]), wait=wle1, inc=sc)
                ev_o_tok[h] = kb.emit("scalar", lambda e, bank=bank, h=h:
                    e.copy(out=den_all[:, h * CA:(h + 1) * CA],
                           in_=PB[bank][64:65, 0:CA]), inc=sc)

            for h in range(8):
                s_mms(h)
                if h >= 1:
                    av_mms(h - 1)
            av_mms(7)
            o_done = sc.tok()
            prev_exp_last = sc.tok()
            prev_odone = o_done

            t_r = kb.emit("vector", lambda e: e.reciprocal(out=den_all[:],
                    in_=den_all[:]), wait=[o_done], inc=ve)
            on_toks = []
            for h in range(8):
                g = h // 2
                hr0 = (h % 2) * 64
                wb = [t_r] if h == 0 else [on_toks[-1]]
                t_b = kb.emit("tensor", lambda e, h=h, hr0=hr0: e.matmul(
                    PB[4][hr0:hr0 + 64, 0:CA], onesr[0:1, 0:64],
                    den_all[:, h * CA:(h + 1) * CA], start=True, stop=True),
                    wait=wb, inc=te)
                on_toks.append(kb.emit("vector", lambda e, g=g, hr0=hr0:
                    e.tensor_tensor(out=Obuf[hr0:hr0 + 64, g * CA:(g + 1) * CA],
                        in0=Obuf[hr0:hr0 + 64, g * CA:(g + 1) * CA],
                        in1=PB[4][hr0:hr0 + 64, 0:CA], op=ALU.mult),
                    wait=[t_b], inc=ve))
            o_norm = ve.tok()
            t_h = kb.emit("gpsimd", lambda e: e.tensor_copy(out=Oh[:], in_=Obuf[:]),
                          wait=[o_norm], inc=gpc, inc_n=1)
            prev_o_gp = gpc.tok()
            kb.emit("vector", lambda e: e.tensor_tensor(out=Ol[:], in0=Obuf[:],
                    in1=Oh[:], op=ALU.subtract), wait=[prev_o_gp], inc=ve)
            o_split = ve.tok()
            prev_o_ve = o_split

            wo_toks = []
            for tt in range(2):
                for half in range(2):
                    bank = 2 + tt * 2 + half
                    def wofn(e, tt=tt, half=half, bank=bank):
                        last = None
                        for kc in range(4):
                            ls = slice(kc * CA + tt * 128, kc * CA + tt * 128 + 128)
                            ws = slice(kc * D + half * 512, kc * D + (half + 1) * 512)
                            last = e.matmul(PB[bank][:, 0:512], Oh[:, ls], woh[:, ws],
                                            start=(kc == 0), stop=False)
                            last = e.matmul(PB[bank][:, 0:512], Ol[:, ls], woh[:, ws],
                                            start=False, stop=False)
                            last = e.matmul(PB[bank][:, 0:512], Oh[:, ls], wol[:, ws],
                                            start=False, stop=(kc == 3))
                        return last
                    wlw = [o_split, t_wo] if (tt == 0 and half == 0) else []
                    t_mm = kb.emit("tensor", wofn, wait=wlw, inc=te)
                    dsl = slice(tt * D + half * 512, tt * D + (half + 1) * 512)
                    wo_toks.append(kb.emit("vector", lambda e, dsl=dsl, bank=bank,
                        p=p: e.tensor_tensor(out=y_tm[p][:, dsl],
                            in0=PB[bank][:, 0:512], in1=y_tm[p][:, dsl],
                            op=ALU.add), wait=[t_mm, car_tok], inc=ve))
            ymid_done = ve.tok()
            for tt in range(2):
                kb.dma("gpsimd", ymid_d[l][q0 + tt * 128: q0 + (tt + 1) * 128, :],
                       y_tm[p][:, tt * D:(tt + 1) * D],
                       wait=[ymid_done] if tt == 0 else (), inc=s_st[p])
            yguard[p] = [ymid_done, s_st[p].tok()]
            prev_attn_te = te.tok()

    if "a0" in phases:
        emit_attn(0, "xn1")

    ar_cur[0] = 0
    ym_tm = [AR(128, 4 * D) for _ in range(2)]
    h_tm = AR(128, 4 * D)
    haff = AR(128, 512)
    hbh = AR(128, NCC * 512, BF16)
    hbl = AR(128, NCC * 512, BF16)
    M1_END = ar_cur[0]
    ar_cur[0] = 0
    w1ah = AR(128, NCC * 512, BF16)
    w1al = AR(128, NCC * 512, BF16)
    w1gh = AR(128, NCC * 512, BF16)
    w1gl = AR(128, NCC * 512, BF16)
    w2h_t = AR(128, 4 * D, BF16)
    w2l_t = AR(128, 4 * D, BF16)
    Gm = AR(128, 4 * 512)
    Am = AR(128, 4 * 512)
    Mh = AR(128, 4 * 512, BF16)
    Ml = AR(128, 4 * 512, BF16)
    ytmp = AR(128, 4 * D)
    lhh = [AR(128, NCC * 512, BF16) for _ in range(2)]
    lhl = [AR(128, NCC * 512, BF16) for _ in range(2)]
    ar_cur[0] = max(M1_END, ar_cur[0])
    lngf = AR(128, 2 * NCC)
    lnbf = AR(128, 2 * NCC)
    b1af = AR(128, 2 * 22)
    b1gf = AR(128, 2 * 22)
    b2tm = AR(128, 2 * D)
    pa = AR(128, 4 * D)
    pb = AR(128, 4 * D)
    s_sh = [kb.sem("s_sh0"), kb.sem("s_sh1")]
    s_acv = [kb.sem(f"s_ac{i}") for i in range(NCH)]
    s_hl = [kb.sem("s_hl0"), kb.sem("s_hl1")]
    extra_barrier_sems.extend(s_acv + s_hl + s_sh)
    NSL = 6

    def emit_mlp(l):
        bar = phase_barrier()
        kb.dma("sync", lngf[:], lng_d[:], wait=bar, inc=s_const)
        kb.dma("sync", lnbf[:], lnb_d[:], inc=s_const)
        kb.dma("sync", b1af[:], b1a_d[:], inc=s_const)
        kb.dma("sync", b1gf[:], b1g_d[:], inc=s_const)
        kb.dma("sync", b2tm[:], b2tm_d[:], inc=s_const)
        c3 = s_const.tok()

        ymg = {0: [], 1: []}
        h_free = []
        hb_free = []
        haff_free = []
        for i in range(NCH):
            p = i % 2
            r0 = i * C
            wl0 = [c3] + ymg[p]
            for tt in range(NT):
                kb.dma("sync", ym_tm[p][:, tt * D:(tt + 1) * D],
                       ymid_d[l][r0 + tt * 128: r0 + (tt + 1) * 128, :],
                       wait=wl0 if tt == 0 else (), inc=s_x[p])
            ld = s_x[p].tok()
            ymv = ym_tm[p].rearrange("q (t d) -> q t d", t=NT)
            hv = h_tm.rearrange("q (t d) -> q t d", t=NT)
            t_c1 = kb.emit("vector", lambda e, ymv=ymv, hv=hv: e.tensor_copy(
                out=hv[:, :, 0:512], in_=ymv[:, :, 0:512]),
                wait=[ld] + h_free, inc=ve)
            sh_toks = [t_c1]
            for tt in range(NT):
                kb.dma("gpsimd", h_tm[1:128, tt * D + 512:(tt + 1) * D],
                       ym_tm[p][0:127, tt * D + 512:(tt + 1) * D],
                       wait=([ld, t_c1] + h_free) if tt == 0 else (), inc=s_sh[p])
                if tt > 0:
                    kb.dma("gpsimd", h_tm[0:1, tt * D + 512:(tt + 1) * D],
                           ym_tm[p][127:128, (tt - 1) * D + 512:tt * D],
                           inc=s_sh[p])
            if i == 0:
                t_z = kb.emit("vector", lambda e: e.memset(
                    h_tm[0:1, 512:1024], 0.0), wait=[t_c1], inc=ve)
                sh_toks.append(t_z)
            else:
                tok_ro = kb.dma("gpsimd", h_tm[0:1, 512:1024],
                                ym_tm[1 - p][127:128, 3 * D + 512:4 * D],
                                wait=[], inc=s_sh[p])
                ymg[1 - p] = ymg[1 - p] + [s_sh[p].tok()]
            sh_toks.append(s_sh[p].tok())
            ln_done = ln_tm(h_tm, h_tm, sh_toks, 1e-5)
            evb = {0: None, 1: None}
            for c in range(NCC):
                b = c % 2
                wlt = [ln_done] if c == 0 else []
                if i > 0 and c < 2:
                    wlt = wlt + h_free
                if evb[b] is not None:
                    wlt.append(evb[b])
                def trh(e, c=c, b=b):
                    last = None
                    for tt in range(NT):
                        last = e.transpose(PB[b][:, tt * 128:(tt + 1) * 128],
                                           h_tm[:, tt * D + c * 128: tt * D + (c + 1) * 128],
                                           id_t[:])
                    return last
                t_tr = kb.emit("tensor", trh, wait=wlt, inc=te)
                t_af = kb.emit("scalar", lambda e, c=c, b=b, l=l: e.activation(
                    out=haff[:], in_=PB[b][:, 0:512], func=AF.Identity,
                    bias=lnbf[:, l * NCC + c:l * NCC + c + 1],
                    scale=lngf[:, l * NCC + c:l * NCC + c + 1]),
                    wait=[t_tr] + haff_free + (hb_free if c == 0 else []), inc=sc)
                t_hi = kb.emit("gpsimd", lambda e, c=c: e.tensor_copy(
                    out=hbh[:, c * 512:(c + 1) * 512], in_=haff[:]),
                    wait=[t_af], inc=gpc, inc_n=1)
                evb[b] = kb.emit("vector", lambda e, c=c: e.tensor_tensor(
                    out=hbl[:, c * 512:(c + 1) * 512], in0=haff[:],
                    in1=hbh[:, c * 512:(c + 1) * 512], op=ALU.subtract),
                    wait=[t_hi], inc=ve)
                kb.dma("gpsimd", hh_d[c * 128:(c + 1) * 128, r0:r0 + C],
                       hbh[:, c * 512:(c + 1) * 512], wait=[evb[b]], inc=s_st[p])
                kb.dma("gpsimd", hl_d[c * 128:(c + 1) * 128, r0:r0 + C],
                       hbl[:, c * 512:(c + 1) * 512], inc=s_st[p])
                haff_free = [t_hi, evb[b]]
            h_free = [te.tok(), ve.tok()]
            hb_free = [s_st[p].tok()]
            ymg[p] = [ve.tok(), s_sh[p].tok()]
        hp_done = [s_st[0].tok(), s_st[1].tok(), te.tok(), ve.tok(), sc.tok(),
                   gpc.tok()]

        w_free = hp_done
        ytmp_free = []
        acc_tok = {}
        lh_guard = {0: hp_done, 1: hp_done}
        prev_w2evac = None
        prev_msplit = []
        for j in range(NSL):
            hid0 = j * 512
            wid = min(512, 2730 - hid0)
            setup_toks = []
            if wid < 512:
                t_ms = kb.emit("vector", lambda e: e.memset(w1ah[:], 0.0),
                               wait=w_free, inc=ve)
                for buf in (w1al, w1gh, w1gl, w2h_t, w2l_t):
                    t_ms = kb.emit("vector", lambda e, buf=buf: e.memset(buf[:], 0.0),
                                   inc=ve)
                setup_toks = [t_ms]
            def lsplit(dst_h, dst_l, get_src, nkc, cols, wait0, bstride=512):
                toks = []
                for kc in range(nkc):
                    sb = kc % NSLAB
                    wl = [slab_free[sb]] + (wait0 if kc < 2 else [])
                    rows, src = get_src(kc)
                    if rows <= 0:
                        continue
                    kb.dma("sync", wtmp[sb][0:rows, 0:cols], src,
                           wait=wl, inc=s_w[sb])
                    t_l = s_w[sb].tok()
                    csl = slice(kc * bstride, kc * bstride + cols)
                    kb.emit("gpsimd", lambda e, sb=sb, csl=csl, rows=rows,
                        cols=cols, dst_h=dst_h: e.tensor_copy(
                            out=dst_h[0:rows, csl], in_=wtmp[sb][0:rows, 0:cols]),
                        wait=[t_l], inc=gpc, inc_n=1)
                    t_h = gpc.tok()
                    t_s = kb.emit("vector", lambda e, sb=sb, csl=csl, rows=rows,
                        cols=cols, dst_h=dst_h, dst_l=dst_l: e.tensor_tensor(
                            out=dst_l[0:rows, csl], in0=wtmp[sb][0:rows, 0:cols],
                            in1=dst_h[0:rows, csl], op=ALU.subtract),
                        wait=[t_h], inc=ve)
                    slab_free[sb] = t_s
                    toks.append(t_s)
                return toks[-1]
            w0 = list(w_free) + setup_toks
            t_w1a = lsplit(w1ah, w1al,
                           lambda kc: (128, w1_d[l, kc * 128:(kc + 1) * 128,
                                               hid0:hid0 + wid]),
                           NCC, wid, w0)
            t_w1g = lsplit(w1gh, w1gl,
                           lambda kc: (128, w1_d[l, kc * 128:(kc + 1) * 128,
                                               2730 + hid0:2730 + hid0 + wid]),
                           NCC, wid, w0)
            def w2src(kc2):
                r0_ = hid0 + kc2 * 128
                rows = max(0, min(128, 2730 - r0_))
                if rows == 0:
                    return (0, None)
                return (rows, w2_d[l, r0_:r0_ + rows, :])
            t_w2 = lsplit(w2h_t, w2l_t, w2src, 4, D, w0, bstride=D)
            w_toks = [t_w1a, t_w1g, t_w2]

            for i in range(NCH):
                p = i % 2
                r0 = i * C
                for kc in range(NCC):
                    kb.dma("sync", lhh[p][:, kc * 512:(kc + 1) * 512],
                           hh_d[kc * 128:(kc + 1) * 128, r0:r0 + C],
                           wait=(lh_guard[p] + w_toks) if kc == 0 else (),
                           inc=s_hl[p])
                    kb.dma("sync", lhl[p][:, kc * 512:(kc + 1) * 512],
                           hl_d[kc * 128:(kc + 1) * 128, r0:r0 + C],
                           inc=s_hl[p])
                t_lh = s_hl[p].tok()
                sp_toks = []
                for b in range(4):
                    pair_toks = []
                    for part, bank0, wh_, wl_ in (("a", 0, w1ah, w1al),
                                                  ("g", 4, w1gh, w1gl)):
                        def w1fn(e, b=b, bank0=bank0, wh_=wh_, wl_=wl_, p=p):
                            last = None
                            for kc in range(NCC):
                                lsl = slice(kc * 512 + b * 128,
                                            kc * 512 + (b + 1) * 128)
                                rs = slice(kc * 512, (kc + 1) * 512)
                                last = e.matmul(PB[bank0 + b][:, 0:512],
                                                wh_[:, lsl], lhh[p][:, rs],
                                                start=(kc == 0), stop=False)
                                last = e.matmul(PB[bank0 + b][:, 0:512],
                                                wh_[:, lsl], lhl[p][:, rs],
                                                start=False, stop=False)
                                last = e.matmul(PB[bank0 + b][:, 0:512],
                                                wl_[:, lsl], lhh[p][:, rs],
                                                start=False, stop=(kc == NCC - 1))
                            return last
                        wlm = []
                        if part == "a" and b == 0:
                            wlm = [t_lh] + ([prev_w2evac] if prev_w2evac else [])
                        pair_toks.append(kb.emit("tensor", w1fn, wait=wlm, inc=te))
                    bcol = min(l * 22 + (hid0 // 128) + b, l * 22 + 21)
                    bs = slice(b * 512, (b + 1) * 512)
                    t_g = kb.emit("scalar", lambda e, b=b, bcol=bcol, bs=bs:
                        e.activation(out=Gm[:, bs], in_=PB[4 + b][:, 0:512],
                            func=AF.Gelu, bias=b1gf[:, bcol:bcol + 1]),
                        wait=[pair_toks[-1]] + (prev_msplit if b == 0 else []),
                        inc=sc)
                    t_a = kb.emit("scalar", lambda e, b=b, bcol=bcol, bs=bs:
                        e.activation(out=Am[:, bs], in_=PB[b][:, 0:512],
                            func=AF.Identity, bias=b1af[:, bcol:bcol + 1]),
                        inc=sc)
                    t_m = kb.emit("vector", lambda e, bs=bs: e.tensor_tensor(
                        out=Am[:, bs], in0=Am[:, bs], in1=Gm[:, bs],
                        op=ALU.mult), wait=[t_a], inc=ve)
                    t_mh = kb.emit("gpsimd", lambda e, bs=bs: e.tensor_copy(
                        out=Mh[:, bs], in_=Am[:, bs]), wait=[t_m],
                        inc=gpc, inc_n=1)
                    sp_toks.append(kb.emit("vector", lambda e, bs=bs:
                        e.tensor_tensor(out=Ml[:, bs], in0=Am[:, bs],
                            in1=Mh[:, bs], op=ALU.subtract), wait=[t_mh], inc=ve))
                w1_done = te.tok()
                t_ml = sp_toks[-1]
                prev_msplit = [t_ml]
                w2e_toks = []
                for tt in range(NT):
                    for half in range(2):
                        bank = tt * 2 + half
                        def w2fn(e, tt=tt, half=half, bank=bank):
                            last = None
                            for kc2 in range(4):
                                lsl = slice(kc2 * 512 + tt * 128,
                                            kc2 * 512 + (tt + 1) * 128)
                                rs = slice(kc2 * D + half * 512,
                                           kc2 * D + (half + 1) * 512)
                                last = e.matmul(PB[bank][:, 0:512], Mh[:, lsl],
                                                w2h_t[:, rs], start=(kc2 == 0),
                                                stop=False)
                                last = e.matmul(PB[bank][:, 0:512], Ml[:, lsl],
                                                w2h_t[:, rs], start=False,
                                                stop=False)
                                last = e.matmul(PB[bank][:, 0:512], Mh[:, lsl],
                                                w2l_t[:, rs], start=False,
                                                stop=(kc2 == 3))
                            return last
                        wlm = [t_ml] if (tt == 0 and half == 0) else []
                        t_mm = kb.emit("tensor", w2fn, wait=wlm, inc=te)
                        dsl = slice(tt * D + half * 512, tt * D + (half + 1) * 512)
                        wle = [t_mm] + (ytmp_free if (tt == 0 and half == 0) else [])
                        w2e_toks.append(kb.emit("scalar", lambda e, dsl=dsl,
                            bank=bank: e.copy(out=ytmp[:, dsl],
                                in_=PB[bank][:, 0:512]), wait=wle, inc=sc))
                prev_w2evac = sc.tok()
                wl_ac = [prev_w2evac]
                if j > 0:
                    wl_ac.append(acc_tok[i])
                for tt in range(NT):
                    kb.dma("gpsimd", yacc_d[l][r0 + tt * 128: r0 + (tt + 1) * 128, :],
                           ytmp[:, tt * D:(tt + 1) * D],
                           wait=wl_ac if tt == 0 else (), inc=s_acv[i],
                           accum_op=(ALU.bypass if j == 0 else ALU.add))
                acc_tok[i] = s_acv[i].tok()
                ytmp_free = [acc_tok[i]]
                lh_guard[p] = [w1_done]
            w_free = [te.tok()]
        for i in range(NCH):
            r0 = i * C
            for tt in range(NT):
                kb.dma("gpsimd", yacc_d[l][r0 + tt * 128: r0 + (tt + 1) * 128, :],
                       b2tm[:, l * D:(l + 1) * D],
                       wait=[acc_tok[i], c3] if tt == 0 else (), inc=s_acv[i],
                       accum_op=ALU.add)
            acc_tok[i] = s_acv[i].tok()

    def emit_p4():
        bar = phase_barrier()
        g4 = []
        for i in range(NCH):
            p = i % 2
            r0 = i * C
            wl0 = bar + g4 + [s_acv[i].tok()]
            for tt in range(NT):
                kb.dma("sync", pa[:, tt * D:(tt + 1) * D],
                       ymid_d[1][r0 + tt * 128: r0 + (tt + 1) * 128, :],
                       wait=wl0 if tt == 0 else (), inc=s_x[p])
            for tt in range(NT):
                kb.dma("sync", pb[:, tt * D:(tt + 1) * D],
                       yacc_d[1][r0 + tt * 128: r0 + (tt + 1) * 128, :],
                       inc=s_x[p])
            for tt in range(NT):
                kb.dma("sync", ytmp[:, tt * D:(tt + 1) * D],
                       s1_d[r0 + tt * 128: r0 + (tt + 1) * 128, :],
                       inc=s_x[p])
            ld = s_x[p].tok()
            t1 = kb.emit("vector", lambda e: e.tensor_tensor(out=pa[:],
                    in0=pa[:], in1=pb[:], op=ALU.add), wait=[ld], inc=ve)
            t3 = kb.emit("scalar", lambda e: e.activation(out=pa[:],
                    in_=pa[:], func=AF.Silu), wait=[t1], inc=sc)
            t4 = kb.emit("vector", lambda e: e.tensor_tensor(out=ytmp[:],
                    in0=ytmp[:], in1=pa[:], op=ALU.mult), wait=[t3], inc=ve)
            for tt in range(NT):
                kb.dma("sync", pb[:, tt * D:(tt + 1) * D],
                       x1_d[r0 + tt * 128: r0 + (tt + 1) * 128, :],
                       wait=[t1] if tt == 0 else (), inc=s_x[p])
            ld2 = s_x[p].tok()
            t5 = kb.emit("vector", lambda e: e.tensor_tensor(out=ytmp[:],
                    in0=ytmp[:], in1=pb[:], op=ALU.add), wait=[t4, ld2], inc=ve)
            for tt in range(NT):
                kb.dma("gpsimd", out_d[r0 + tt * 128: r0 + (tt + 1) * 128, :],
                       ytmp[:, tt * D:(tt + 1) * D],
                       wait=[t5] if tt == 0 else (), inc=s_st[p])
            g4 = [t5, s_st[0].tok(), s_st[1].tok()]

    if "m0" in phases:
        emit_mlp(0)
    if "a1" in phases:
        emit_attn(1, "prev")
    if "m1" in phases:
        emit_mlp(1)
    if "p4" in phases:
        emit_p4()

    kb.emit("gpsimd", lambda e: e.engine_nop(),
            wait=[s_st[0].tok(), s_st[1].tok()])

    kb.finalize()
    return nc


import functools

@functools.lru_cache(maxsize=4)
def _rots(S):
    f32 = np.float32
    DH = 64
    half = np.arange(0, DH, 2, dtype=f32)
    inv_freq = (1.0 / 10000.0 ** (half / DH)).astype(f32)
    t = np.arange(S, dtype=f32)
    freqs = t[:, None] * inv_freq[None, :]
    freqs = np.concatenate([freqs, freqs], -1)
    base = ((half + 0.4 * DH) / (1.4 * DH)).astype(f32)
    power = (t - S // 2) / 256.0
    scale = (base[None, :] ** power[:, None]).astype(f32)
    scale = np.concatenate([scale, scale], -1)
    cos, sin = np.cos(freqs, dtype=f32), np.sin(freqs, dtype=f32)
    tabs = ((cos * scale).astype(f32), (sin * scale).astype(f32),
            (cos / scale).astype(f32), (sin / scale).astype(f32))
    return tuple(np.ascontiguousarray(np.tile(tb.T, (2, 1)).astype(f32))
                 for tb in tabs)

def _hsel():
    m = np.zeros((128, 32), np.float32)
    for g in range(4):
        for p in range(128):
            m[p, g * 8 + 2 * g + p // 64] = 1.0
    return m

def _mbc():
    m = np.zeros((8, 4 * 128), np.float32)
    for g in range(4):
        for j_ in range(128):
            m[2 * g + j_ // 64, g * 128 + j_] = 1.0
    return m

def _prot():
    m = np.zeros((128, 128), np.float32)
    for i in range(128):
        hh, d_ = i // 64, i % 64
        sign = -1.0 if d_ < 32 else 1.0
        m[hh * 64 + (d_ + 32) % 64, i] = sign
    return m

def _dupcol(qs):
    qs = np.asarray(qs, np.float32)
    return np.ascontiguousarray(np.concatenate([qs.T, qs.T], 0))

def _b1fm(inp, off):
    b1 = np.asarray(inp["L1_ffn_b1"], np.float32)
    out = np.zeros((128, 2 * 22), np.float32)
    for l_ in range(2):
        for blk in range(22):
            seg = b1[l_, off + blk * 128: off + min(blk * 128 + 128, off and 0 or 0) ] if False else                   b1[l_, off + blk * 128: min(off + blk * 128 + 128, off + 2730)]
            out[0:len(seg), l_ * 22 + blk] = seg
    return np.ascontiguousarray(out)

def _mask01():
    p = np.arange(128)[:, None]
    w = np.arange(1152)[None, :]
    return ((w - p >= 384) & (w - p <= 896)).astype(np.float32)

def _lorart(inp):
    f32 = np.float32
    m = np.zeros((D, 40), f32)
    m[:, 0:8] = np.asarray(inp["L0_ffn_lora_A"], f32)
    m[:, 32:40] = np.asarray(inp["L0_ffn_router_w"], f32).T
    return np.ascontiguousarray(m)


def host_inputs(inp, core):
    f32 = np.float32
    S = np.asarray(inp["L0_attn_x"]).shape[1]
    asf = lambda a: np.ascontiguousarray(np.asarray(a, f32))
    m1v = np.zeros((8, 64), f32)
    m2v = np.zeros((8, 64), f32)
    for e_ in range(8):
        for r_ in range(8):
            m1v[r_, e_ * 8 + r_] = 1.0
            m2v[e_, e_ * 8 + r_] = 1.0
    return {
        "x": asf(np.asarray(inp["L0_attn_x"])[core]),
        "Wg": asf(inp["L0_attn_Wg"]),
        "Wv": asf(inp["L0_attn_Wv"]),
        "Wd": asf(inp["L0_attn_Wd"]),
        "baseT": asf(np.asarray(inp["L0_ffn_base_w"], f32).T),
        "lorart": _lorart(inp),
        "Bp": asf(np.asarray(inp["L0_ffn_lora_B"], f32).reshape(64, D)),
        "bg_fm": asf(np.asarray(inp["L0_attn_bg"], f32).reshape(NCC, 128).T),
        "bv_fm": asf(np.asarray(inp["L0_attn_bv"], f32).reshape(NCC, 128).T),
        "bd_fm": asf(np.asarray(inp["L0_attn_bd"], f32).reshape(NCC, 128).T),
        "bb_fm": asf(np.asarray(inp["L0_ffn_base_b"], f32).reshape(NCC, 128).T),
        "rb_col": asf(np.asarray(inp["L0_ffn_router_b"], f32).reshape(8, 1)),
        "ident": np.eye(128, dtype=f32),
        "m1": m1v,
        "m2": m2v,
        "cvec": np.tile(np.asarray(CONST_VALS, f32), (128, 1)),
        "WrT": asf(np.asarray(inp["L1_attn_W_r"], f32).T),
        "WiT": asf(np.asarray(inp["L1_attn_W_i"], f32).T),
        "WoT": asf(np.asarray(inp["L1_attn_W_out"], f32).T),
        "br_fm": asf(np.asarray(inp["L1_attn_b_r"], f32).reshape(NCC, 128).T),
        "bi_fm": asf(np.asarray(inp["L1_attn_b_i"], f32).reshape(NCC, 128).T),
        "convw_fm": asf(np.asarray(inp["L1_attn_conv_w"], f32)[:, 0, :]
                        .reshape(NCC, 128, 4).transpose(1, 0, 2).reshape(128, 4 * NCC)),
        "cb_fm": asf(np.asarray(inp["L1_attn_conv_b"], f32).reshape(NCC, 128).T),
        "loga_fm": asf(np.asarray(inp["L1_attn_log_a"], f32).reshape(NCC, 128).T),
        "gamma_tm": asf(np.tile(np.asarray(inp["L1_attn_gamma"], f32), (128, 1))),
        "wqkv": asf(inp["L1_ffn_wqkv"]),
        "wo": asf(inp["L1_ffn_wo"]),
        "w1": asf(inp["L1_ffn_w1"]),
        "w2": asf(inp["L1_ffn_w2"]),
        "hsel": _hsel(),
        "mbc": _mbc(),
        "prot": _prot(),
        "qsd": _dupcol(inp["L1_ffn_q_scale"]),
        "qsp": _dupcol(inp["L1_ffn_q_scale"]),
        "ksd": _dupcol(inp["L1_ffn_k_scale"]),
        "ksp": _dupcol(inp["L1_ffn_k_scale"]),
        "rotA": _rots(S)[0], "rotB": _rots(S)[1],
        "rotC": _rots(S)[2], "rotD": _rots(S)[3],
        "mask01": _mask01(),
        "lng_fm": asf(np.asarray(inp["L1_ffn_ln_g"], f32).reshape(2 * NCC, 128).T),
        "lnb_fm": asf(np.asarray(inp["L1_ffn_ln_b"], f32).reshape(2 * NCC, 128).T),
        "b1a_fm": _b1fm(inp, 0),
        "b1g_fm": _b1fm(inp, 2730),
        "b2tm": asf(np.tile(np.asarray(inp["L1_ffn_b2"], f32).reshape(1, 2 * D), (128, 1))),
    }


_CACHE = {}


def kernel(**inputs):
    from concourse.bass_utils import run_bass_kernel_spmd
    S = int(np.asarray(inputs["L0_attn_x"]).shape[1])
    if S not in _CACHE:
        _CACHE[S] = build_program(S=S, C=512)
    nc = _CACHE[S]
    in_maps = [host_inputs(inputs, core) for core in range(8)]
    res = run_bass_kernel_spmd(nc, in_maps, core_ids=list(range(8)))
    return np.stack([np.asarray(r["out"]) for r in res.results], axis=0)
